# version 27
# baseline (speedup 1.0000x reference)
import contextlib
import copy
import numpy as np
import concourse.bass as bass
import concourse.mybir as mybir
import concourse.bass_utils as _bass_utils
from concourse.bass_utils import run_bass_kernel_spmd

_orig_run_command = _bass_utils.run_command


def _run_command_ldw(cmd, *args, **kwargs):
    cmd = ["--enable-ldw-opt=true" if c == "--enable-ldw-opt=false" else c
           for c in cmd]
    return _orig_run_command(cmd, *args, **kwargs)


_bass_utils.run_command = _run_command_ldw

F32 = mybir.dt.float32
BF16 = mybir.dt.bfloat16
I32 = mybir.dt.int32
U8 = mybir.dt.uint8
AF = mybir.ActivationFunctionType
ALU = mybir.AluOpType
AXX = mybir.AxisListType.X

S_OWN = 2048
D = 2048
H = 16
HD = 128
FI = 8192
NT = S_OWN // 128
NCH = S_OWN // 512

INV_SQRT_HD = float(1.0 / np.sqrt(np.float32(HD)))
LN_EPS = 1e-5
MAGIC = 12582912.0
TWO_PI = 2.0 * np.pi
C1 = 6.28125
C2 = float(np.float32(TWO_PI - C1))
C3 = float(TWO_PI - C1 - np.float64(np.float32(TWO_PI - C1)))
INV_2PI = float(1.0 / TWO_PI)
PI_HALF = float(np.pi / 2)
SIN_HI = float(np.pi) - 1e-3
SIN_LO = -SIN_HI
COS_HI = float(np.pi / 2) - 1e-3
COS_LO = float(-3 * np.pi / 2) + 1e-3

ENGS = ["sync", "scalar", "vector", "tensor", "gpsimd"]
STREAMS = ["d_c", "d_w", "d_x0", "d_x1", "d_q", "d_k", "d_v", "d_qr", "d_kr",
           "d_vr", "d_fc0", "d_fc1", "d_fc2", "d_fc3", "d_fcs", "d_y1",
           "d_y1r", "d_fg", "d_fd", "d_w1", "d_w2", "d_w3",
           "d_out", "d_wo", "d_x2", "d_fu"]


class Em:

    def __init__(self, nc, ctx):
        self.nc = nc
        self.sems = {n: ctx.enter_context(nc.semaphore(n))
                     for n in ENGS + STREAMS}
        self.clk = {n: 0 for n in ENGS + STREAMS}
        self.waited = {}
        self.eng = None
        self.h = None

    def snapshot(self):
        return dict(self.clk), dict(self.waited)

    def restore(self, snap):
        self.clk.clear(); self.clk.update(snap[0])
        self.waited.clear(); self.waited.update(snap[1])

    def _waits(self, eng, waits):
        for sem_name, val in waits:
            if val <= 0:
                continue
            key = (eng, sem_name)
            if self.waited.get(key, 0) >= val:
                continue
            self.waited[key] = val
            if self.eng == eng and self.h is not None:
                self.h.wait_ge(self.sems[sem_name], val)

    def do(self, eng, opname, *args, waits=(), **kw):
        self._waits(eng, waits)
        self.clk[eng] += 1
        if self.eng == eng:
            ins = getattr(self.h, opname)(*args, **kw)
            ins.then_inc(self.sems[eng], 1)
        return self.clk[eng]

    def mms(self, group, waits=()):
        self._waits("tensor", waits)
        self.clk["tensor"] += 1
        if self.eng == "tensor":
            for args, kw in group:
                ins = self.h.matmul(*args, **kw)
            ins.then_inc(self.sems["tensor"], 1)
        return self.clk["tensor"]

    def transposes(self, items, waits=()):
        self._waits("tensor", waits)
        self.clk["tensor"] += 1
        if self.eng == "tensor":
            for out, in_, ident in items:
                ins = self.h.transpose(out, in_, ident)
            ins.then_inc(self.sems["tensor"], 1)
        return self.clk["tensor"]

    def dma(self, eng, stream, out, in_, waits=(), serialize=True):
        if serialize:
            waits = list(waits) + [(stream, self.clk[stream])]
        self._waits(eng, waits)
        self.clk[stream] += 16
        if self.eng == eng:
            ins = self.h.dma_start(out=out, in_=in_)
            ins.then_inc(self.sems[stream], 16)
        return self.clk[stream]

    def wait(self, consumer_eng, waits):
        self._waits(consumer_eng, waits)


def build(debug=False):
    nc = bass.Bass("TRN2", target_bir_lowering=False)

    x_own = nc.dram_tensor("x_own", [S_OWN, D], F32, kind="ExternalInput").ap()
    wkT_p = nc.dram_tensor("wkT_p", [D, D], F32, kind="ExternalInput").ap()
    wvT = nc.dram_tensor("wvT", [D, D], F32, kind="ExternalInput").ap()
    wqT_p = nc.dram_tensor("wqT_p", [D, D], F32, kind="ExternalInput").ap()
    woT = nc.dram_tensor("woT", [D, D], F32, kind="ExternalInput").ap()
    gwT = nc.dram_tensor("gwT", [D, FI], U8, kind="ExternalInput").ap()
    uwT = nc.dram_tensor("uwT", [D, FI], U8, kind="ExternalInput").ap()
    dwT = nc.dram_tensor("dwT", [FI, D], U8, kind="ExternalInput").ap()
    gscale = nc.dram_tensor("gscale", [128, FI // 128], F32, kind="ExternalInput").ap()
    uscale = nc.dram_tensor("uscale", [128, FI // 128], F32, kind="ExternalInput").ap()
    dscale = nc.dram_tensor("dscale", [128, D // 128], F32, kind="ExternalInput").ap()
    invf_rep = nc.dram_tensor("invf_rep", [1, H * 64], F32, kind="ExternalInput").ap()
    t_ownT = nc.dram_tensor("t_ownT", [1, S_OWN], F32, kind="ExternalInput").ap()
    ident_bf = nc.dram_tensor("ident_bf", [128, 128], BF16, kind="ExternalInput").ap()
    onesc = nc.dram_tensor("onesc", [128, 4], F32, kind="ExternalInput").ap()

    out_rows = nc.dram_tensor("out_rows", [S_OWN, D], F32, kind="ExternalOutput").ap()

    q_sc = nc.dram_tensor("q_sc", [S_OWN, D], BF16)
    k_sc = nc.dram_tensor("k_sc", [S_OWN, D], BF16)
    v_sc = nc.dram_tensor("v_sc", [S_OWN, D], BF16)
    y1_sc = nc.dram_tensor("y1_sc", [S_OWN, D], F32)
    xn2_sc = nc.dram_tensor("xn2_sc", [16, 128, S_OWN], BF16)
    wg_sc = nc.dram_tensor("wg_sc", [16, 128, FI], BF16)
    wu_sc = nc.dram_tensor("wu_sc", [16, 128, FI], BF16)
    wd_sc = nc.dram_tensor("wd_sc", [64, 128, D], BF16)

    dbg = {}
    if debug:
        dbg["q"] = nc.dram_tensor("dbg_q", [S_OWN, D], BF16, kind="ExternalOutput").ap()
        dbg["at"] = nc.dram_tensor("dbg_at", [S_OWN, D], BF16, kind="ExternalOutput").ap()
        dbg["y1"] = nc.dram_tensor("dbg_y1", [S_OWN, D], F32, kind="ExternalOutput").ap()

    with contextlib.ExitStack() as ctx:
        sb = lambda *a: ctx.enter_context(nc.sbuf_tensor(*a))
        em = Em(nc, ctx)

        ident_s = sb("ident_s", [128, 128], BF16)
        consts_s = sb("consts_s", [128, 4], F32)
        invfr_s = sb("invfr_s", [1, H * 64], F32)
        t_sb = sb("t_sb", [1, S_OWN], F32)
        gsc_s = sb("gsc_s", [128, FI // 128], F32)
        usc_s = sb("usc_s", [128, FI // 128], F32)
        dsc_s = sb("dsc_s", [128, D // 128], F32)
        st_sx = sb("st_sx", [128, 8], F32)
        st_sxx = sb("st_sxx", [128, 8], F32)
        st_m = sb("st_m", [128, 8], F32)
        st_msq = sb("st_msq", [128, 8], F32)
        st_var = sb("st_var", [128, 8], F32)
        st_lnv = sb("st_lnv", [128, 8], F32)
        st_rstd = sb("st_rstd", [128, 8], F32)
        st_negmr = sb("st_negmr", [128, 8], F32)

        block = ctx.enter_context(nc.Block())
        M = {}

        def run_phase(fn, *args):
            snap = em.snapshot()
            msnap = copy.deepcopy(M)

            def mk(name):
                def emit(e):
                    em.restore(snap)
                    M.clear()
                    M.update(copy.deepcopy(msnap))
                    em.eng = name
                    em.h = e
                    fn(*args)
                return emit
            block.sync(mk("sync"))
            block.scalar(mk("scalar"))
            block.vector(mk("vector"))
            block.tensor(mk("tensor"))
            block.gpsimd(mk("gpsimd"))

        def barrier(streams=()):
            def phase_barrier():
                allw = ([(e, em.clk[e])
                         for e in ("scalar", "vector", "tensor")] +
                        [(stn, em.clk[stn]) for stn in streams])
                for e in ENGS:
                    em.wait(e, allw)
            run_phase(phase_barrier)

        def phase0():
            for ap_d, tile in [(ident_bf, ident_s), (onesc, consts_s),
                               (invf_rep, invfr_s), (t_ownT, t_sb),
                               (gscale, gsc_s), (uscale, usc_s),
                               (dscale, dsc_s)]:
                c = em.dma("gpsimd", "d_c", tile[:], ap_d)
            M["consts"] = c

        run_phase(phase0)

        def emit_ln_tile(xrow_t, slot, dst_bf, sq_scr, w_extra):
            c = lambda col: (slice(None), slice(col, col + 1))
            cast = em.do("scalar", "activation", dst_bf[:], xrow_t, AF.Identity,
                         accum_out=st_sx[c(slot)], waits=w_extra)
            sq = em.do("vector", "scalar_tensor_tensor", sq_scr[:], dst_bf[:], 1.0,
                       dst_bf[:], ALU.mult, ALU.mult, accum_out=st_sxx[c(slot)],
                       waits=[("scalar", cast)])
            m = em.do("vector", "tensor_scalar", st_m[c(slot)], st_sx[c(slot)],
                      1.0 / D, None, ALU.mult, waits=[("vector", sq)])
            msq = em.do("vector", "tensor_tensor", st_msq[c(slot)], st_m[c(slot)],
                        st_m[c(slot)], ALU.mult, waits=[("vector", m)])
            var = em.do("vector", "scalar_tensor_tensor", st_var[c(slot)],
                        st_sxx[c(slot)], 1.0 / D, st_msq[c(slot)],
                        ALU.mult, ALU.subtract, waits=[("vector", msq)])
            lnv = em.do("scalar", "activation", st_lnv[c(slot)], st_var[c(slot)],
                        AF.Ln, bias=consts_s[:, 1:2],
                        waits=[("vector", var), ("d_c", M["consts"])])
            rstd = em.do("scalar", "activation", st_rstd[c(slot)], st_lnv[c(slot)],
                         AF.Exp, scale=-0.5, waits=[("scalar", lnv)])
            negmr = em.do("vector", "scalar_tensor_tensor", st_negmr[c(slot)],
                          st_m[c(slot)], -1.0, st_rstd[c(slot)], ALU.mult, ALU.mult,
                          waits=[("scalar", rstd)])
            apply = em.do("scalar", "activation", dst_bf[:], xrow_t, AF.Identity,
                          scale=st_rstd[c(slot)], bias=st_negmr[c(slot)],
                          waits=[("vector", negmr)])
            return apply

        cv_sb = contextlib.ExitStack()
        cvin = [cv_sb.enter_context(nc.sbuf_tensor(f"cvin{i}", [128, 1024], U8))
                for i in range(4)]
        cvout = [cv_sb.enter_context(nc.sbuf_tensor(f"cvout{i}", [128, 1024], BF16))
                 for i in range(4)]

        CONV_ROUNDS = []
        for _mat, _cache in [(gwT, wg_sc), (uwT, wu_sc), (dwT, wd_sc)]:
            for _rh in range(_mat.shape[0] // 128):
                for _cb in range(_mat.shape[1] // 1024):
                    CONV_ROUNDS.append((_mat, _cache, _rh, _cb))
        N_CONV = len(CONV_ROUNDS)

        def conv_step(n=1):
            for _ in range(n):
                r = M["cv_r"]
                if r >= N_CONV:
                    return
                M["cv_r"] = r + 1
                mat_ap, cache, rh, cb = CONV_ROUNDS[r]
                b = r % 4
                ld = em.dma("sync", f"d_fc{b}", cvin[b][:],
                            mat_ap[rh * 128:(rh + 1) * 128,
                                   cb * 1024:(cb + 1) * 1024],
                            waits=[("scalar", M["cv_m"][b])])
                cv = em.do("scalar", "activation", cvout[b][:], cvin[b][:],
                           AF.Copy, bias=-128.0,
                           waits=[(f"d_fc{b}", ld),
                                  ("d_fcs", M["cv_st"][b])])
                M["cv_m"][b] = cv
                st = em.dma("scalar", "d_fcs",
                            cache.ap()[rh:rh + 1, :,
                                       cb * 1024:(cb + 1) * 1024].rearrange(
                                "o p c -> p (o c)"),
                            cvout[b][:], waits=[("scalar", cv)])
                M["cv_st"][b] = st
                M["conv_done"] = st

        M.update(cv_r=0, cv_m=[0] * 4, cv_st=[0] * 4, conv_done=0)

        xnT_scope = contextlib.ExitStack()
        xnT = xnT_scope.enter_context(
            nc.sbuf_tensor("xnT", [128, 16 * S_OWN], BF16))
        wst = xnT_scope.enter_context(
            nc.sbuf_tensor("wst", [128, 16 * D], BF16))
        W_STREAMS = ["d_w", "d_w1", "d_w2", "d_w3"]
        M.update(wld=[0, 0, 0, 0], w_use=0)

        def emit_wst_loads(src_ap, first_waits=()):
            for dc in range(16):
                M["wld"][dc % 4] = em.dma(
                    "gpsimd", W_STREAMS[dc % 4],
                    wst.ap()[:, dc * D:(dc + 1) * D],
                    src_ap[dc * 128:(dc + 1) * 128, :],
                    waits=list(first_waits) if dc < 4 else ())
        pa_sb = contextlib.ExitStack()
        sba = lambda *a: pa_sb.enter_context(nc.sbuf_tensor(*a))
        xrow = [sba(f"xrow{i}", [128, 2 * D], F32) for i in range(2)]
        xbf = [sba(f"xbf{i}", [128, D], BF16) for i in range(2)]
        sqd = xbf

        pa_ps = contextlib.ExitStack()
        ps_tp = pa_ps.enter_context(nc.psum_tensor("ps_tp", [128, 2048], BF16))

        M.update(x_apply={0: 0, 1: 0}, xld={0: 0, 1: 0}, tp_unit=0, tp_ev=0)

        def phaseA():
            emit_wst_loads(wqT_p)
            for g in range(8):
                conv_step(4)
                bi = g % 2
                xb = xrow[bi]
                M["xld"][bi] = em.dma(
                    "sync", f"d_x{bi}", xb[:],
                    x_own[g * 256:(g + 1) * 256, :].rearrange(
                        "(st p) d -> p st d", p=128),
                    waits=[("scalar", M["x_apply"][bi])])
                for t2 in range(2):
                    t = g * 2 + t2
                    w_extra = [(f"d_x{bi}", M["xld"][bi]),
                               ("tensor", M["tp_unit"])]
                    apply = emit_ln_tile(xb[:, t2 * D:(t2 + 1) * D], t % 4,
                                         xbf[t % 2], sqd[t % 2], w_extra)
                    if t2 == 1:
                        M["x_apply"][bi] = apply
                    items = [(ps_tp[:, dc * 128:(dc + 1) * 128],
                              xbf[t % 2][:, dc * 128:(dc + 1) * 128], ident_s[:])
                             for dc in range(16)]
                    tp = em.transposes(items, waits=[
                        ("scalar", apply), ("vector", M["tp_ev"]),
                        ("d_c", M["consts"])])
                    M["tp_unit"] = tp
                    src = ps_tp.ap().rearrange("p (dc q) -> p dc q", dc=16)
                    dstv = xnT.ap().rearrange("p (dc sl) -> p dc sl", dc=16)
                    ev = em.do("vector", "tensor_copy",
                               dstv[:, :, t * 128:(t + 1) * 128], src,
                               waits=[("tensor", tp)])
                    M["tp_ev"] = ev
            M["xnT_done"] = M["tp_ev"]

        run_phase(phaseA)
        barrier(["d_x0", "d_x1"])
        pa_ps.close()
        pa_sb.close()

        pb_sb = contextlib.ExitStack()
        sbb = lambda *a: pb_sb.enter_context(nc.sbuf_tensor(*a))
        prow = [sbb(f"prow{i}", [128, D], BF16) for i in range(2)]

        pb_ps = contextlib.ExitStack()
        ps_p = [pb_ps.enter_context(nc.psum_tensor(f"ps_p{i}", [128, 512], F32))
                for i in range(2)]

        M.update(prow_st=[0, 0], pev=[0, 0])

        def phaseB(src_ap, dst_sc, stream, prev_stream, preloaded=False):
            prevm = list(M["prow_st"])
            M["prow_st"] = [0, 0]
            if not preloaded:
                emit_wst_loads(src_ap, [("tensor", M["w_use"])])
            last_mm = 0
            for st in range(NT):
                conv_step(3)
                bi = st % 2
                for oc in range(4):
                    mm = em.mms(
                        [((ps_p[oc % 2][:],
                           xnT[:, dc * S_OWN + st * 128:dc * S_OWN + st * 128 + 128],
                           wst[:, dc * D + oc * 512:dc * D + (oc + 1) * 512]),
                          dict(start=(dc == 0), stop=(dc == 15)))
                         for dc in range(16)],
                        waits=[("d_w", M["wld"][0]), ("d_w1", M["wld"][1]),
                               ("d_w2", M["wld"][2]), ("d_w3", M["wld"][3]),
                               ("vector", M["xnT_done"]),
                               ("scalar", M["pev"][oc % 2])])
                    last_mm = mm
                    ev = em.do("scalar", "activation",
                               prow[bi][:, oc * 512:(oc + 1) * 512],
                               ps_p[oc % 2][:], AF.Identity,
                               waits=[("tensor", mm)] +
                                     ([(stream, M["prow_st"][bi]),
                                       (prev_stream, prevm[bi])] if oc == 0
                                      else []))
                    M["pev"][oc % 2] = ev
                M["prow_st"][bi] = em.dma(
                    "sync", stream, dst_sc.ap()[st * 128:(st + 1) * 128, :],
                    prow[bi][:], waits=[("scalar", ev)])
            M["w_use"] = last_mm

        M["prow_st"] = [0, 0]
        run_phase(phaseB, wqT_p, q_sc, "d_q", "d_c", True)
        M["qst_done"] = em.clk["d_q"]
        run_phase(phaseB, wkT_p, k_sc, "d_k", "d_q")
        M["kst_done"] = em.clk["d_k"]
        run_phase(phaseB, wvT, v_sc, "d_v", "d_k")
        M["vst_done"] = em.clk["d_v"]

        barrier(["d_w", "d_q", "d_k", "d_v"])
        pb_ps.close()
        pb_sb.close()
        xnT_scope.close()

        pc_sb = contextlib.ExitStack()
        sbc = lambda *a: pc_sb.enter_context(nc.sbuf_tensor(*a))
        qt = [sbc(f"qt{i}", [128, D], BF16) for i in range(2)]
        kt = [sbc(f"kt{i}", [128, D], BF16) for i in range(2)]
        vt = [sbc(f"vt{i}", [128, D], BF16) for i in range(2)]
        sc_sb = sbc("sc_sb", [128, 256], F32)
        p_sb = sbc("p_sb", [128, 256], F32)
        pbf = sbc("pbf", [128, 256], BF16)
        den_sb = sbc("den_sb", [128, 16], F32)
        rden_sb = sbc("rden_sb", [128, 16], F32)
        NPROD = 4
        prod = [sbc(f"prod{i}", [128, 16 * 128], BF16) for i in range(NPROD)]
        vre = sbc("vre", [128, 2048], BF16)
        red32 = sbc("red32", [128, 2048], F32)
        arow = sbc("arow", [128, D], BF16)
        aTs = [sbc(f"aTs{i}", [128, 16 * 128], BF16) for i in range(2)]
        wo_bf = sbc("wo_bf", [128, 16 * D], BF16)
        yrow_t = sbc("yrow_t", [128, D], BF16)
        xr4 = [sbc(f"xr4_{i}", [128, D], F32) for i in range(2)]
        y1t = xr4
        xn2bf = [sbc(f"xn2bf{i}", [128, D], BF16) for i in range(2)]
        sqd4 = xn2bf
        xst_sb = [sbc(f"xst_sb{i}", [128, 2048], BF16) for i in range(1)] * 2

        pc_ps = contextlib.ExitStack()
        ps_tpc = pc_ps.enter_context(nc.psum_tensor("ps_tpc", [128, 2048], BF16))
        ps_tpd = pc_ps.enter_context(nc.psum_tensor("ps_tpd", [128, 2048], BF16))
        ps_y = [pc_ps.enter_context(nc.psum_tensor(f"ps_y{i}", [128, 512], F32))
                for i in range(2)]
        M.update(qld=[0, 0], kld=[0, 0], vld=[0, 0], ang_rd=0, tab_sc=0,
                 sc_done=0, att_done=[0, 0],
                 exp_done=0, p_used=0, p_used_g=0, pbf_used_v=0,
                 scp_done_p=[0, 0], scp_done_vv=[0, 0],
                 prod_rd=[0] * NPROD, vre_used=[0, 0], vre_cp=0,
                 atg_last=0, atv_last=0, arow_cast=0,
                 arow_tp=0, att_last_cur=0, atp_ev=[0, 0], atp_ev_last=0,
                 wo_last=[0, 0], y1use_v=0,
                 yev=[0, 0], y1st=[0, 0], x4ld=[0, 0], x4use=[0, 0],
                 tpd_ev=0, tpd_unit=0, x2st=[0, 0])

        def bc_ap(t, offset, dims):
            return bass.AP(tensor=t.tensor if hasattr(t, "tensor") else t,
                           offset=offset,
                           ap=[[t.ap().ap[0][0], 128]] + [list(d) for d in dims])

        POOL_SC = 14
        POOL_AT = 14

        def phaseCD():
            for ic in range(16):
                M["wold"] = em.dma("gpsimd", "d_wo",
                                  wo_bf.ap()[:, ic * D:(ic + 1) * D],
                                  woT[ic * 128:(ic + 1) * 128, :],
                                  waits=[("tensor", M["w_use"])] if ic == 0
                                  else ())
            def stage0(st):
                bi = st % 2
                M["qld"][bi] = em.dma(
                    "scalar", "d_qr", qt[bi][:],
                    q_sc.ap()[st * 128:(st + 1) * 128, :],
                    waits=[("gpsimd", M["scp_done_p"][bi]),
                           ("vector", M["scp_done_vv"][bi]),
                           ("d_q", M["qst_done"]),
                           ("d_k", M["kst_done"]), ("d_v", M["vst_done"])])
                M["kld"][bi] = em.dma(
                    "scalar", "d_kr", kt[bi][:],
                    k_sc.ap()[st * 128:(st + 1) * 128, :],
                    waits=[("gpsimd", M["scp_done_p"][bi]),
                           ("vector", M["scp_done_vv"][bi])])
                M["vld"][bi] = em.dma(
                    "scalar", "d_vr", vt[bi][:],
                    v_sc.ap()[st * 128:(st + 1) * 128, :],
                    waits=[("gpsimd", M["vre_used"][bi])])
            def stage1(st):
                bi = st % 2
                M["x4ld"][bi] = em.dma(
                    "scalar", f"d_x{bi}", xr4[bi][:],
                    x_own[st * 128:(st + 1) * 128, :],
                    waits=[("scalar", M["x4use"][bi]),
                           ("d_y1", M["y1st"][bi])])
                kr3 = kt[bi].ap().rearrange("p (t d) -> p t d", t=16)
                pm = [("gpsimd", 0)] * NPROD
                scpg = scpv = 0
                for hh in range(H):
                    pb = hh % NPROD
                    peng = "gpsimd" if hh < POOL_SC else "vector"
                    qbc = bc_ap(qt[bi], hh * 128, [[0, 16], [1, 128]])
                    pv = em.do(
                        peng, "tensor_tensor",
                        prod[pb].ap().rearrange("p (t d) -> p t d", t=16),
                        qbc, kr3, ALU.mult,
                        waits=[("d_qr", M["qld"][bi]), ("d_kr", M["kld"][bi]),
                               ("vector", M["prod_rd"][pb])])
                    pm[pb] = (peng, pv)
                    if peng == "gpsimd":
                        scpg = pv
                    else:
                        scpv = pv
                    M["prod_rd"][pb] = em.do(
                        "vector", "tensor_reduce",
                        sc_sb[:, hh * H:(hh + 1) * H],
                        prod[pb].ap().rearrange("p (t d) -> p t d", t=16),
                        axis=AXX, op=ALU.add,
                        waits=[pm[pb], ("scalar", M["exp_done"])])
                last_sc = max(M["prod_rd"])
                M["sc_done"] = last_sc
                M["vre_used"][bi] = M["vre_cp"] = em.do(
                    "gpsimd", "tensor_copy", vre[:],
                    bc_ap(vt[bi], 0, [[1, 128], [128, 16]]),
                    waits=[("d_vr", M["vld"][bi]),
                           ("gpsimd", M["atg_last"]),
                           ("vector", M["atv_last"])])
                ex = em.do("scalar", "activation", p_sb[:], sc_sb[:], AF.Exp,
                           scale=INV_SQRT_HD,
                           waits=[("vector", last_sc),
                                  ("vector", M["p_used"])])
                M["exp_done"] = ex
                dn = em.do("vector", "tensor_reduce", den_sb[:],
                           p_sb.ap().rearrange("p (hh t) -> p hh t", hh=H),
                           axis=AXX, op=ALU.add, waits=[("scalar", ex)])
                rd = em.do("vector", "reciprocal", rden_sb[:], den_sb[:],
                           waits=[("vector", dn)])
                p3 = p_sb.ap().rearrange("p (hh t) -> p hh t", hh=H)
                pb3 = pbf.ap().rearrange("p (hh t) -> p hh t", hh=H)
                pn = em.do("vector", "scalar_tensor_tensor", pb3, p3, 1.0,
                           bc_ap(rden_sb, 0, [[1, 16], [0, 16]]),
                           ALU.mult, ALU.mult,
                           waits=[("vector", rd), ("gpsimd", M["p_used_g"]),
                                  ("vector", M["pbf_used_v"])])
                atg = atv = 0
                red_last = pn
                for hh in range(H):
                    pb = hh % NPROD
                    peng = "gpsimd" if hh < POOL_AT else "vector"
                    pbc = bc_ap(pbf, hh * 16, [[0, 128], [1, 16]])
                    vv = vre.ap().rearrange("p (d t) -> p d t", d=128)
                    pv = em.do(
                        peng, "tensor_tensor",
                        prod[pb].ap().rearrange("p (d t) -> p d t", d=128),
                        pbc, vv, ALU.mult,
                        waits=[("vector", pn), ("vector", M["prod_rd"][pb]),
                               ("gpsimd", M["vre_cp"])])
                    if peng == "gpsimd":
                        atg = pv
                    else:
                        atv = pv
                    M["prod_rd"][pb] = red_last = em.do(
                        "vector", "tensor_reduce",
                        red32[:, hh * 128:(hh + 1) * 128],
                        prod[pb].ap().rearrange("p (d t) -> p d t", d=128),
                        axis=AXX, op=ALU.add,
                        waits=[(peng, pv), ("scalar", M["arow_cast"])])
                acast = em.do("scalar", "activation", arow[:], red32[:],
                              AF.Identity,
                              waits=[("vector", red_last),
                                     ("tensor", M["arow_tp"])])
                M["arow_cast"] = acast
                att_last = red_last
                M["p_used"] = pn
                M["p_used_g"] = atg
                M["pbf_used_v"] = atv
                M["atg_last"] = atg
                M["atv_last"] = atv
                M["scp_done_p"][bi] = max(scpg, atg)
                M["scp_done_vv"][bi] = max(scpv, atv)
                M["att_done"][bi] = att_last
                M["att_last_cur"] = acast
                conv_step(6)

            def stage1b(st):
                bi = st % 2
                items = [(ps_tpc[:, dc * 128:(dc + 1) * 128],
                          arow[:, dc * 128:(dc + 1) * 128], ident_s[:])
                         for dc in range(16)]
                tp = em.transposes(items, waits=[
                    ("scalar", M["att_last_cur"]),
                    ("scalar", M["atp_ev_last"])])
                M["arow_tp"] = tp
                ev = em.do("scalar", "activation", aTs[bi][:], ps_tpc[:],
                           AF.Copy,
                           waits=[("tensor", tp), ("tensor", M["wo_last"][bi])])
                M["atp_ev"][bi] = ev
                M["atp_ev_last"] = ev

            def stage2(s):
                bi = s % 2
                ev4 = 0
                last_womm = 0
                for oc in range(4):
                    mm = em.mms(
                        [((ps_y[oc % 2][:],
                           aTs[bi][:, ic * 128:(ic + 1) * 128],
                           wo_bf[:, ic * D + oc * 512:ic * D + (oc + 1) * 512]),
                          dict(start=(ic == 0), stop=(ic == 15)))
                         for ic in range(16)],
                        waits=[("d_wo", M["wold"]), ("scalar", M["atp_ev"][bi]),
                               ("scalar", M["yev"][oc % 2])])
                    last_womm = mm
                    ev4 = em.do("scalar", "activation",
                                yrow_t[:, oc * 512:(oc + 1) * 512],
                                ps_y[oc % 2][:], AF.Identity,
                                waits=[("tensor", mm),
                                       ("vector", M["y1use_v"])])
                    M["yev"][oc % 2] = ev4
                M["wo_last"][bi] = last_womm
                y1 = em.do("vector", "tensor_tensor", xr4[bi][:], xr4[bi][:],
                           yrow_t[:], ALU.add,
                           waits=[("scalar", ev4),
                                  (f"d_x{bi}", M["x4ld"][bi])])
                M["y1use_v"] = y1
                M["y1st"][bi] = em.dma(
                    "scalar", "d_y1", y1_sc.ap()[s * 128:(s + 1) * 128, :],
                    y1t[bi][:], waits=[("vector", y1)])
                apply = emit_ln_tile(y1t[bi][:], 4 + s % 2, xn2bf[bi], sqd4[bi],
                                     [("vector", y1), ("tensor", M["tpd_unit"])])
                M["x4use"][bi] = apply
                items = [(ps_tpd[:, dc * 128:(dc + 1) * 128],
                          xn2bf[bi][:, dc * 128:(dc + 1) * 128], ident_s[:])
                         for dc in range(16)]
                tp2 = em.transposes(items, waits=[
                    ("scalar", apply), ("scalar", M["tpd_ev"])])
                M["tpd_unit"] = tp2
                ev2 = em.do("scalar", "activation", xst_sb[bi][:], ps_tpd[:],
                            AF.Copy,
                            waits=[("tensor", tp2), ("d_x2", M["x2st"][0]),
                                   ("d_x2", M["x2st"][1])])
                M["tpd_ev"] = ev2
                M["x2st"][bi] = em.dma(
                    "scalar", "d_x2",
                    xn2_sc.ap()[:, :, s * 128:(s + 1) * 128].rearrange(
                        "dc p sl -> p dc sl"),
                    xst_sb[bi].ap().rearrange("p (dc sl) -> p dc sl", dc=16),
                    waits=[("scalar", ev2)])
                if debug:
                    em.dma("gpsimd", "d_x2",
                           dbg["y1"][s * 128:(s + 1) * 128, :], y1t[bi][:],
                           waits=[("vector", y1)])
                conv_step(6)

            stage0(0)
            for it in range(NT + 1):
                if it + 1 < NT:
                    stage0(it + 1)
                if it < NT:
                    stage1(it)
                if it >= 1:
                    stage2(it - 1)
                if it < NT:
                    stage1b(it)
            M["xn2_done"] = M["tpd_ev"]

        run_phase(phaseCD)
        barrier(["d_qr", "d_kr", "d_vr", "d_wo", "d_x0", "d_x1", "d_y1"])
        pc_ps.close()
        pc_sb.close()

        pe_sb = contextlib.ExitStack()
        sbe = lambda *a: pe_sb.enter_context(nc.sbuf_tensor(*a))
        xn2q = sbe("xn2q", [128, 16 * 512], BF16)
        h_sb = sbe("h_sb", [128, 64 * 512], BF16)
        yf_sb = sbe("yf_sb", [128, 16 * 512], BF16)
        gu_sb = sbe("gu_sb", [128, 512], BF16)
        uu_sb = sbe("uu_sb", [128, 512], BF16)
        wgb = [sbe(f"wgb{i}", [128, 16 * 128], BF16) for i in range(2)]
        wub = [sbe(f"wub{i}", [128, 16 * 128], BF16) for i in range(2)]
        wdb = [sbe(f"wdb{i}", [128, 32 * 128], BF16) for i in range(2)]
        y1r = [sbe(f"y1r{i}", [128, D], F32) for i in range(2)]
        outt = [sbe(f"outt{i}", [128, D], F32) for i in range(2)]

        pe_ps = contextlib.ExitStack()
        pste = lambda *a: pe_ps.enter_context(nc.psum_tensor(*a))
        ps_g = pste("ps_g", [128, 512], F32)
        ps_u = pste("ps_u", [128, 512], F32)
        ps_dn = [pste(f"ps_dn{i}", [128, 512], F32) for i in range(2)]
        ps_tpe = pste("ps_tpe", [128, 2048], BF16)

        M.update(g_ev=0, u_ev=0, hmul=0, wg_use=[0, 0], wd_use=[0, 0],
                 wu_use=[0, 0], xn2q_use=0,
                 dn_ev=[0, 0], yf_last=0, yftp=0, ytp_rd=0, h_used=0,
                 y1ld=[0, 0], out_tt=0, outst=[0, 0], yf_used=0)

        def phase_conv_drain():
            conv_step(N_CONV)

        run_phase(phase_conv_drain)

        def phaseE(qc):
            x2l = em.dma("sync", "d_y1r",
                         xn2q.ap().rearrange("p (dc sl) -> p dc sl", dc=16),
                         xn2_sc.ap()[:, :, qc * 512:(qc + 1) * 512].rearrange(
                             "dc p sl -> p dc sl"),
                         waits=[("tensor", M["xn2q_use"]),
                                ("d_x2", M["x2st"][0]), ("d_x2", M["x2st"][1])])
            for it in range(64):
                wb = it % 2
                wgl = em.dma("sync", "d_fg",
                             wgb[wb].ap().rearrange("p (hc i) -> p hc i", hc=16),
                             wg_sc.ap()[:, :, it * 128:(it + 1) * 128].rearrange(
                                 "hc p i -> p hc i"),
                             waits=[("tensor", M["wg_use"][wb]),
                                    ("d_fcs", M["conv_done"])])
                gmm = em.mms(
                    [((ps_g[:], wgb[wb][:, hc * 128:(hc + 1) * 128],
                       xn2q[:, hc * 512:(hc + 1) * 512]),
                      dict(start=(hc == 0), stop=(hc == 15)))
                     for hc in range(16)],
                    waits=[("d_fg", wgl), ("d_y1r", x2l),
                           ("scalar", M["g_ev"])])
                gev = em.do("scalar", "activation", gu_sb[:], ps_g[:],
                            AF.Silu, scale=gsc_s[:, it:it + 1],
                            waits=[("tensor", gmm), ("vector", M["hmul"]),
                                   ("d_c", M["consts"])])
                M["g_ev"] = gev
                wul = em.dma("sync", "d_fu",
                             wub[wb].ap().rearrange("p (hc i) -> p hc i", hc=16),
                             wu_sc.ap()[:, :, it * 128:(it + 1) * 128].rearrange(
                                 "hc p i -> p hc i"),
                             waits=[("tensor", M["wu_use"][wb]),
                                    ("d_fcs", M["conv_done"])])
                umm = em.mms(
                    [((ps_u[:], wub[wb][:, hc * 128:(hc + 1) * 128],
                       xn2q[:, hc * 512:(hc + 1) * 512]),
                      dict(start=(hc == 0), stop=(hc == 15)))
                     for hc in range(16)],
                    waits=[("d_fu", wul), ("scalar", M["u_ev"])])
                M["wg_use"][wb] = gmm
                M["wu_use"][wb] = umm
                uev = em.do("scalar", "activation", uu_sb[:], ps_u[:],
                            AF.Identity, scale=usc_s[:, it:it + 1],
                            waits=[("tensor", umm), ("vector", M["hmul"])])
                M["u_ev"] = uev
                hm = em.do("vector", "tensor_tensor",
                           h_sb[:, it * 512:(it + 1) * 512], gu_sb[:], uu_sb[:],
                           ALU.mult,
                           waits=[("scalar", uev)] +
                                 ([("tensor", M["h_used"])] if it == 0 else []))
                M["hmul"] = hm
            h_done = M["hmul"]
            dn_last = 0
            for ht in range(16):
                wb = ht % 2
                wdl1 = em.dma("sync", "d_fd",
                              wdb[wb].ap().rearrange("p (ic i) -> p ic i", ic=32),
                              wd_sc.ap()[0:32, :, ht * 128:(ht + 1) * 128].rearrange(
                                  "ic p hh -> p ic hh"),
                              waits=[("tensor", M["wd_use"][wb])])
                dmm = em.mms(
                    [((ps_dn[ht % 2][:], wdb[wb][:, ic * 128:(ic + 1) * 128],
                       h_sb[:, ic * 512:(ic + 1) * 512]),
                      dict(start=(ic == 0), stop=False, skip_group_check=True))
                     for ic in range(32)],
                    waits=[("d_fd", wdl1), ("vector", h_done),
                           ("scalar", M["dn_ev"][ht % 2])])
                wdl2 = em.dma("sync", "d_fd",
                              wdb[wb].ap().rearrange("p (ic i) -> p ic i", ic=32),
                              wd_sc.ap()[32:64, :, ht * 128:(ht + 1) * 128].rearrange(
                                  "ic p hh -> p ic hh"),
                              waits=[("tensor", dmm)])
                dmm = em.mms(
                    [((ps_dn[ht % 2][:], wdb[wb][:, ic * 128:(ic + 1) * 128],
                       h_sb[:, (32 + ic) * 512:(33 + ic) * 512]),
                      dict(start=False, stop=(ic == 31), skip_group_check=True))
                     for ic in range(32)],
                    waits=[("d_fd", wdl2)])
                M["wd_use"][wb] = dmm
                dn_last = dmm
                yfe = em.do("scalar", "activation",
                            yf_sb[:, ht * 512:(ht + 1) * 512],
                            ps_dn[ht % 2][:], AF.Identity,
                            scale=dsc_s[:, ht:ht + 1],
                            waits=[("tensor", dmm), ("tensor", M["yftp"])])
                M["dn_ev"][ht % 2] = yfe
                M["yf_last"] = yfe
            M["h_used"] = dn_last
            M["xn2q_use"] = M["wu_use"][1]
            for qs in range(4):
                items = [(ps_tpe[:, ht * 128:(ht + 1) * 128],
                          yf_sb[:, ht * 512 + qs * 128:ht * 512 + (qs + 1) * 128],
                          ident_s[:]) for ht in range(16)]
                tp = em.transposes(items, waits=[
                    ("scalar", M["yf_last"]), ("vector", M["ytp_rd"])])
                M["yftp"] = tp
                bi = qs % 2
                r0 = qc * 512 + qs * 128
                yl = em.dma("sync", "d_y1r", y1r[bi][:],
                            y1_sc.ap()[r0:r0 + 128, :],
                            waits=[("vector", M["y1ld"][bi]),
                                   ("d_y1", M["y1st"][0]),
                                   ("d_y1", M["y1st"][1])])
                ott = em.do("vector", "tensor_tensor", outt[bi][:], y1r[bi][:],
                            ps_tpe[:], ALU.add,
                            waits=[("tensor", tp), ("d_y1r", yl),
                                   ("d_out", M["outst"][bi])])
                M["y1ld"][bi] = ott
                M["ytp_rd"] = ott
                M["out_tt"] = ott
                M["outst"][bi] = em.dma(
                    "gpsimd", "d_out", out_rows[r0:r0 + 128, :], outt[bi][:],
                    waits=[("vector", ott)])

        for qc in range(NCH):
            run_phase(phaseE, qc)

        def phase_final():
            em.wait("gpsimd", [(stn, em.clk[stn]) for stn in STREAMS])
            em.wait("sync", [("d_out", em.clk["d_out"])])

        run_phase(phase_final)
        pe_ps.close()
        pe_sb.close()
        cv_sb.close()

    return nc



def _lowbyte(a):
    return np.ascontiguousarray(
        a.view(np.uint8).reshape(a.shape[0], a.shape[1], 4)[:, :, 0])


def _stage_inputs(inputs):
    import ml_dtypes
    x = np.ascontiguousarray(np.asarray(inputs["L0_attn_x"], dtype=np.float32)[0])
    wq = np.asarray(inputs["L0_attn_wq"], dtype=np.float32)
    wk = np.asarray(inputs["L0_attn_wk"], dtype=np.float32)
    wv = np.asarray(inputs["L0_attn_wv"], dtype=np.float32)
    wo = np.asarray(inputs["L0_attn_wo"], dtype=np.float32)
    inv_freq = np.asarray(inputs["L0_attn_inv_freq"], dtype=np.float32)
    gw = np.asarray(inputs["L0_ffn_gate_w"], dtype=np.int32)
    gs = np.asarray(inputs["L0_ffn_gate_scale"], dtype=np.float32)
    uw = np.asarray(inputs["L0_ffn_up_w"], dtype=np.int32)
    us = np.asarray(inputs["L0_ffn_up_scale"], dtype=np.float32)
    dw = np.asarray(inputs["L0_ffn_down_w"], dtype=np.int32)
    ds = np.asarray(inputs["L0_ffn_down_scale"], dtype=np.float32)

    perm = np.concatenate([np.arange(0, HD, 2), np.arange(1, HD, 2)])
    perm_g = (np.arange(H)[:, None] * HD + perm[None, :]).reshape(-1)

    common = {
        "wkT_p": np.ascontiguousarray(wk.T[:, perm_g]),
        "wqT_p": np.ascontiguousarray(wq.T[:, perm_g]),
        "wvT": np.ascontiguousarray(wv.T),
        "woT": np.ascontiguousarray(wo.T),
        "gwT": _lowbyte(np.ascontiguousarray(gw.T)),
        "uwT": _lowbyte(np.ascontiguousarray(uw.T)),
        "dwT": _lowbyte(np.ascontiguousarray(dw.T)),
        "gscale": np.ascontiguousarray(gs.reshape(FI // 128, 128).T),
        "uscale": np.ascontiguousarray(us.reshape(FI // 128, 128).T),
        "dscale": np.ascontiguousarray(ds.reshape(D // 128, 128).T),
        "invf_rep": np.ascontiguousarray(np.tile(inv_freq, H)[None, :]),
        "ident_bf": np.eye(128).astype(ml_dtypes.bfloat16),
        "onesc": np.ascontiguousarray(np.broadcast_to(
            np.array([1.0, LN_EPS, 1e-37, PI_HALF], np.float32), (128, 4))),
    }
    per_core = []
    for c in range(8):
        m = dict(common)
        m["x_own"] = np.ascontiguousarray(x[c * S_OWN:(c + 1) * S_OWN])
        m["t_ownT"] = np.arange(c * S_OWN, (c + 1) * S_OWN,
                                dtype=np.float32)[None, :].copy()
        per_core.append(m)
    return per_core


def kernel(**inputs):
    in_maps = _stage_inputs(inputs)
    nc = build(debug=False)
    res = run_bass_kernel_spmd(nc, in_maps, core_ids=list(range(8)))
    out = np.concatenate([res.results[c]["out_rows"] for c in range(8)], axis=0)
    return out[None, :, :]


# version 28
# speedup vs baseline: 1.0316x; 1.0316x over previous
import contextlib
import copy
import numpy as np
import concourse.bass as bass
import concourse.mybir as mybir
import concourse.bass_utils as _bass_utils
from concourse.bass_utils import run_bass_kernel_spmd

_orig_run_command = _bass_utils.run_command


def _run_command_ldw(cmd, *args, **kwargs):
    cmd = ["--enable-ldw-opt=true" if c == "--enable-ldw-opt=false" else c
           for c in cmd]
    return _orig_run_command(cmd, *args, **kwargs)


_bass_utils.run_command = _run_command_ldw

F32 = mybir.dt.float32
BF16 = mybir.dt.bfloat16
I32 = mybir.dt.int32
U8 = mybir.dt.uint8
AF = mybir.ActivationFunctionType
ALU = mybir.AluOpType
AXX = mybir.AxisListType.X

S_OWN = 2048
D = 2048
H = 16
HD = 128
FI = 8192
NT = S_OWN // 128
NCH = S_OWN // 512

INV_SQRT_HD = float(1.0 / np.sqrt(np.float32(HD)))
LN_EPS = 1e-5
MAGIC = 12582912.0
TWO_PI = 2.0 * np.pi
C1 = 6.28125
C2 = float(np.float32(TWO_PI - C1))
C3 = float(TWO_PI - C1 - np.float64(np.float32(TWO_PI - C1)))
INV_2PI = float(1.0 / TWO_PI)
PI_HALF = float(np.pi / 2)
SIN_HI = float(np.pi) - 1e-3
SIN_LO = -SIN_HI
COS_HI = float(np.pi / 2) - 1e-3
COS_LO = float(-3 * np.pi / 2) + 1e-3

ENGS = ["sync", "scalar", "vector", "tensor", "gpsimd"]
STREAMS = ["d_c", "d_w", "d_x0", "d_x1", "d_q", "d_k", "d_v", "d_qr", "d_kr",
           "d_vr", "d_fc0", "d_fc1", "d_fc2", "d_fc3", "d_fcs", "d_y1",
           "d_y1r", "d_fg", "d_fd", "d_w1", "d_w2", "d_w3",
           "d_out", "d_wo", "d_x2", "d_fu"]


class Em:

    def __init__(self, nc, ctx):
        self.nc = nc
        self.sems = {n: ctx.enter_context(nc.semaphore(n))
                     for n in ENGS + STREAMS}
        self.clk = {n: 0 for n in ENGS + STREAMS}
        self.waited = {}
        self.eng = None
        self.h = None

    def snapshot(self):
        return dict(self.clk), dict(self.waited)

    def restore(self, snap):
        self.clk.clear(); self.clk.update(snap[0])
        self.waited.clear(); self.waited.update(snap[1])

    def _waits(self, eng, waits):
        for sem_name, val in waits:
            if val <= 0:
                continue
            key = (eng, sem_name)
            if self.waited.get(key, 0) >= val:
                continue
            self.waited[key] = val
            if self.eng == eng and self.h is not None:
                self.h.wait_ge(self.sems[sem_name], val)

    def do(self, eng, opname, *args, waits=(), **kw):
        self._waits(eng, waits)
        self.clk[eng] += 1
        if self.eng == eng:
            ins = getattr(self.h, opname)(*args, **kw)
            ins.then_inc(self.sems[eng], 1)
        return self.clk[eng]

    def mms(self, group, waits=()):
        self._waits("tensor", waits)
        self.clk["tensor"] += 1
        if self.eng == "tensor":
            for args, kw in group:
                ins = self.h.matmul(*args, **kw)
            ins.then_inc(self.sems["tensor"], 1)
        return self.clk["tensor"]

    def transposes(self, items, waits=()):
        self._waits("tensor", waits)
        self.clk["tensor"] += 1
        if self.eng == "tensor":
            for out, in_, ident in items:
                ins = self.h.transpose(out, in_, ident)
            ins.then_inc(self.sems["tensor"], 1)
        return self.clk["tensor"]

    def dma(self, eng, stream, out, in_, waits=(), serialize=True):
        if serialize:
            waits = list(waits) + [(stream, self.clk[stream])]
        self._waits(eng, waits)
        self.clk[stream] += 16
        if self.eng == eng:
            ins = self.h.dma_start(out=out, in_=in_)
            ins.then_inc(self.sems[stream], 16)
        return self.clk[stream]

    def wait(self, consumer_eng, waits):
        self._waits(consumer_eng, waits)


def build(debug=False):
    nc = bass.Bass("TRN2", target_bir_lowering=False)

    x_own = nc.dram_tensor("x_own", [S_OWN, D], F32, kind="ExternalInput").ap()
    wkT_p = nc.dram_tensor("wkT_p", [D, D], F32, kind="ExternalInput").ap()
    wvT = nc.dram_tensor("wvT", [D, D], F32, kind="ExternalInput").ap()
    wqT_p = nc.dram_tensor("wqT_p", [D, D], F32, kind="ExternalInput").ap()
    woT = nc.dram_tensor("woT", [D, D], F32, kind="ExternalInput").ap()
    gwT = nc.dram_tensor("gwT", [D, FI], U8, kind="ExternalInput").ap()
    uwT = nc.dram_tensor("uwT", [D, FI], U8, kind="ExternalInput").ap()
    dwT = nc.dram_tensor("dwT", [FI, D], U8, kind="ExternalInput").ap()
    gscale = nc.dram_tensor("gscale", [128, FI // 128], F32, kind="ExternalInput").ap()
    uscale = nc.dram_tensor("uscale", [128, FI // 128], F32, kind="ExternalInput").ap()
    dscale = nc.dram_tensor("dscale", [128, D // 128], F32, kind="ExternalInput").ap()
    invf_rep = nc.dram_tensor("invf_rep", [1, H * 64], F32, kind="ExternalInput").ap()
    t_ownT = nc.dram_tensor("t_ownT", [1, S_OWN], F32, kind="ExternalInput").ap()
    ident_bf = nc.dram_tensor("ident_bf", [128, 128], BF16, kind="ExternalInput").ap()
    onesc = nc.dram_tensor("onesc", [128, 4], F32, kind="ExternalInput").ap()

    out_rows = nc.dram_tensor("out_rows", [S_OWN, D], F32, kind="ExternalOutput").ap()

    q_sc = nc.dram_tensor("q_sc", [S_OWN, D], BF16)
    k_sc = nc.dram_tensor("k_sc", [S_OWN, D], BF16)
    v_sc = nc.dram_tensor("v_sc", [S_OWN, D], BF16)
    y1_sc = nc.dram_tensor("y1_sc", [S_OWN, D], F32)
    xn2_sc = nc.dram_tensor("xn2_sc", [16, 128, S_OWN], BF16)
    wg_sc = nc.dram_tensor("wg_sc", [16, 128, FI], BF16)
    wu_sc = nc.dram_tensor("wu_sc", [16, 128, FI], BF16)
    wd_sc = nc.dram_tensor("wd_sc", [64, 128, D], BF16)

    dbg = {}
    if debug:
        dbg["q"] = nc.dram_tensor("dbg_q", [S_OWN, D], BF16, kind="ExternalOutput").ap()
        dbg["at"] = nc.dram_tensor("dbg_at", [S_OWN, D], BF16, kind="ExternalOutput").ap()
        dbg["y1"] = nc.dram_tensor("dbg_y1", [S_OWN, D], F32, kind="ExternalOutput").ap()

    with contextlib.ExitStack() as ctx:
        sb = lambda *a: ctx.enter_context(nc.sbuf_tensor(*a))
        em = Em(nc, ctx)

        ident_s = sb("ident_s", [128, 128], BF16)
        consts_s = sb("consts_s", [128, 4], F32)
        invfr_s = sb("invfr_s", [1, H * 64], F32)
        t_sb = sb("t_sb", [1, S_OWN], F32)
        gsc_s = sb("gsc_s", [128, FI // 128], F32)
        usc_s = sb("usc_s", [128, FI // 128], F32)
        dsc_s = sb("dsc_s", [128, D // 128], F32)
        st_sx = sb("st_sx", [128, 8], F32)
        st_sxx = sb("st_sxx", [128, 8], F32)
        st_m = sb("st_m", [128, 8], F32)
        st_msq = sb("st_msq", [128, 8], F32)
        st_var = sb("st_var", [128, 8], F32)
        st_lnv = sb("st_lnv", [128, 8], F32)
        st_rstd = sb("st_rstd", [128, 8], F32)
        st_negmr = sb("st_negmr", [128, 8], F32)

        block = ctx.enter_context(nc.Block())
        M = {}

        def run_phase(fn, *args):
            snap = em.snapshot()
            msnap = copy.deepcopy(M)

            def mk(name):
                def emit(e):
                    em.restore(snap)
                    M.clear()
                    M.update(copy.deepcopy(msnap))
                    em.eng = name
                    em.h = e
                    fn(*args)
                return emit
            block.sync(mk("sync"))
            block.scalar(mk("scalar"))
            block.vector(mk("vector"))
            block.tensor(mk("tensor"))
            block.gpsimd(mk("gpsimd"))

        def barrier(streams=()):
            def phase_barrier():
                allw = ([(e, em.clk[e])
                         for e in ("scalar", "vector", "tensor")] +
                        [(stn, em.clk[stn]) for stn in streams])
                for e in ENGS:
                    em.wait(e, allw)
            run_phase(phase_barrier)

        def phase0():
            for ap_d, tile in [(ident_bf, ident_s), (onesc, consts_s),
                               (invf_rep, invfr_s), (t_ownT, t_sb),
                               (gscale, gsc_s), (uscale, usc_s),
                               (dscale, dsc_s)]:
                c = em.dma("gpsimd", "d_c", tile[:], ap_d)
            M["consts"] = c

        run_phase(phase0)

        def emit_ln_tile(xrow_t, slot, dst_bf, sq_scr, w_extra):
            c = lambda col: (slice(None), slice(col, col + 1))
            cast = em.do("scalar", "activation", dst_bf[:], xrow_t, AF.Identity,
                         accum_out=st_sx[c(slot)], waits=w_extra)
            sq = em.do("vector", "scalar_tensor_tensor", sq_scr[:], dst_bf[:], 1.0,
                       dst_bf[:], ALU.mult, ALU.mult, accum_out=st_sxx[c(slot)],
                       waits=[("scalar", cast)])
            m = em.do("vector", "tensor_scalar", st_m[c(slot)], st_sx[c(slot)],
                      1.0 / D, None, ALU.mult, waits=[("vector", sq)])
            msq = em.do("vector", "tensor_tensor", st_msq[c(slot)], st_m[c(slot)],
                        st_m[c(slot)], ALU.mult, waits=[("vector", m)])
            var = em.do("vector", "scalar_tensor_tensor", st_var[c(slot)],
                        st_sxx[c(slot)], 1.0 / D, st_msq[c(slot)],
                        ALU.mult, ALU.subtract, waits=[("vector", msq)])
            lnv = em.do("scalar", "activation", st_lnv[c(slot)], st_var[c(slot)],
                        AF.Ln, bias=consts_s[:, 1:2],
                        waits=[("vector", var), ("d_c", M["consts"])])
            rstd = em.do("scalar", "activation", st_rstd[c(slot)], st_lnv[c(slot)],
                         AF.Exp, scale=-0.5, waits=[("scalar", lnv)])
            negmr = em.do("vector", "scalar_tensor_tensor", st_negmr[c(slot)],
                          st_m[c(slot)], -1.0, st_rstd[c(slot)], ALU.mult, ALU.mult,
                          waits=[("scalar", rstd)])
            apply = em.do("scalar", "activation", dst_bf[:], xrow_t, AF.Identity,
                          scale=st_rstd[c(slot)], bias=st_negmr[c(slot)],
                          waits=[("vector", negmr)])
            return apply

        cv_sb = contextlib.ExitStack()
        cvin = [cv_sb.enter_context(nc.sbuf_tensor(f"cvin{i}", [128, 1024], U8))
                for i in range(4)]
        cvout = [cv_sb.enter_context(nc.sbuf_tensor(f"cvout{i}", [128, 1024], BF16))
                 for i in range(4)]

        CONV_ROUNDS = []
        for _mat, _cache in [(gwT, wg_sc), (uwT, wu_sc), (dwT, wd_sc)]:
            for _rh in range(_mat.shape[0] // 128):
                for _cb in range(_mat.shape[1] // 1024):
                    CONV_ROUNDS.append((_mat, _cache, _rh, _cb))
        N_CONV = len(CONV_ROUNDS)

        def conv_step(n=1):
            for _ in range(n):
                r = M["cv_r"]
                if r >= N_CONV:
                    return
                M["cv_r"] = r + 1
                mat_ap, cache, rh, cb = CONV_ROUNDS[r]
                b = r % 4
                ld = em.dma("sync", f"d_fc{b}", cvin[b][:],
                            mat_ap[rh * 128:(rh + 1) * 128,
                                   cb * 1024:(cb + 1) * 1024],
                            waits=[("scalar", M["cv_m"][b])])
                cv = em.do("scalar", "activation", cvout[b][:], cvin[b][:],
                           AF.Copy, bias=-128.0,
                           waits=[(f"d_fc{b}", ld),
                                  ("d_fcs", M["cv_st"][b])])
                M["cv_m"][b] = cv
                st = em.dma("scalar", "d_fcs",
                            cache.ap()[rh:rh + 1, :,
                                       cb * 1024:(cb + 1) * 1024].rearrange(
                                "o p c -> p (o c)"),
                            cvout[b][:], waits=[("scalar", cv)])
                M["cv_st"][b] = st
                M["conv_done"] = st

        M.update(cv_r=0, cv_m=[0] * 4, cv_st=[0] * 4, conv_done=0)

        xnT_scope = contextlib.ExitStack()
        xnT = xnT_scope.enter_context(
            nc.sbuf_tensor("xnT", [128, 16 * S_OWN], BF16))
        wst = xnT_scope.enter_context(
            nc.sbuf_tensor("wst", [128, 16 * D], BF16))
        W_STREAMS = ["d_w", "d_w1", "d_w2", "d_w3"]
        M.update(wld=[0, 0, 0, 0], w_use=0)

        def emit_wst_loads(src_ap, first_waits=()):
            for dc in range(16):
                M["wld"][dc % 4] = em.dma(
                    "gpsimd", W_STREAMS[dc % 4],
                    wst.ap()[:, dc * D:(dc + 1) * D],
                    src_ap[dc * 128:(dc + 1) * 128, :],
                    waits=list(first_waits) if dc < 4 else ())
        pa_sb = contextlib.ExitStack()
        sba = lambda *a: pa_sb.enter_context(nc.sbuf_tensor(*a))
        xrow = [sba(f"xrow{i}", [128, 2 * D], F32) for i in range(2)]
        xbf = [sba(f"xbf{i}", [128, D], BF16) for i in range(2)]
        sqd = xbf

        pa_ps = contextlib.ExitStack()
        ps_tp = pa_ps.enter_context(nc.psum_tensor("ps_tp", [128, 2048], BF16))

        M.update(x_apply={0: 0, 1: 0}, xld={0: 0, 1: 0}, tp_unit=0, tp_ev=0)

        def phaseA():
            emit_wst_loads(wqT_p)
            for g in range(8):
                conv_step(4)
                bi = g % 2
                xb = xrow[bi]
                M["xld"][bi] = em.dma(
                    "sync", f"d_x{bi}", xb[:],
                    x_own[g * 256:(g + 1) * 256, :].rearrange(
                        "(st p) d -> p st d", p=128),
                    waits=[("scalar", M["x_apply"][bi])])
                for t2 in range(2):
                    t = g * 2 + t2
                    w_extra = [(f"d_x{bi}", M["xld"][bi]),
                               ("tensor", M["tp_unit"])]
                    apply = emit_ln_tile(xb[:, t2 * D:(t2 + 1) * D], t % 4,
                                         xbf[t % 2], sqd[t % 2], w_extra)
                    if t2 == 1:
                        M["x_apply"][bi] = apply
                    items = [(ps_tp[:, dc * 128:(dc + 1) * 128],
                              xbf[t % 2][:, dc * 128:(dc + 1) * 128], ident_s[:])
                             for dc in range(16)]
                    tp = em.transposes(items, waits=[
                        ("scalar", apply), ("vector", M["tp_ev"]),
                        ("d_c", M["consts"])])
                    M["tp_unit"] = tp
                    src = ps_tp.ap().rearrange("p (dc q) -> p dc q", dc=16)
                    dstv = xnT.ap().rearrange("p (dc sl) -> p dc sl", dc=16)
                    ev = em.do("vector", "tensor_copy",
                               dstv[:, :, t * 128:(t + 1) * 128], src,
                               waits=[("tensor", tp)])
                    M["tp_ev"] = ev
            M["xnT_done"] = M["tp_ev"]

        run_phase(phaseA)
        barrier(["d_x0", "d_x1"])
        pa_ps.close()
        pa_sb.close()

        pb_sb = contextlib.ExitStack()
        sbb = lambda *a: pb_sb.enter_context(nc.sbuf_tensor(*a))
        prow = [sbb(f"prow{i}", [128, D], BF16) for i in range(2)]

        pb_ps = contextlib.ExitStack()
        ps_p = [pb_ps.enter_context(nc.psum_tensor(f"ps_p{i}", [128, 512], F32))
                for i in range(2)]

        M.update(prow_st=[0, 0], pev=[0, 0])

        def phaseB(src_ap, dst_sc, stream, prev_stream, preloaded=False):
            prevm = list(M["prow_st"])
            M["prow_st"] = [0, 0]
            if not preloaded:
                emit_wst_loads(src_ap, [("tensor", M["w_use"])])
            last_mm = 0
            for st in range(NT):
                conv_step(3)
                bi = st % 2
                for oc in range(4):
                    mm = em.mms(
                        [((ps_p[oc % 2][:],
                           xnT[:, dc * S_OWN + st * 128:dc * S_OWN + st * 128 + 128],
                           wst[:, dc * D + oc * 512:dc * D + (oc + 1) * 512]),
                          dict(start=(dc == 0), stop=(dc == 15)))
                         for dc in range(16)],
                        waits=[("d_w", M["wld"][0]), ("d_w1", M["wld"][1]),
                               ("d_w2", M["wld"][2]), ("d_w3", M["wld"][3]),
                               ("vector", M["xnT_done"]),
                               ("scalar", M["pev"][oc % 2])])
                    last_mm = mm
                    ev = em.do("scalar", "activation",
                               prow[bi][:, oc * 512:(oc + 1) * 512],
                               ps_p[oc % 2][:], AF.Identity,
                               waits=[("tensor", mm)] +
                                     ([(stream, M["prow_st"][bi]),
                                       (prev_stream, prevm[bi])] if oc == 0
                                      else []))
                    M["pev"][oc % 2] = ev
                M["prow_st"][bi] = em.dma(
                    "sync", stream, dst_sc.ap()[st * 128:(st + 1) * 128, :],
                    prow[bi][:], waits=[("scalar", ev)])
            M["w_use"] = last_mm

        M["prow_st"] = [0, 0]
        run_phase(phaseB, wqT_p, q_sc, "d_q", "d_c", True)
        M["qst_done"] = em.clk["d_q"]
        run_phase(phaseB, wkT_p, k_sc, "d_k", "d_q")
        M["kst_done"] = em.clk["d_k"]
        run_phase(phaseB, wvT, v_sc, "d_v", "d_k")
        M["vst_done"] = em.clk["d_v"]

        barrier(["d_w", "d_q", "d_k", "d_v"])
        pb_ps.close()
        pb_sb.close()
        xnT_scope.close()

        pc_sb = contextlib.ExitStack()
        sbc = lambda *a: pc_sb.enter_context(nc.sbuf_tensor(*a))
        qt = [sbc(f"qt{i}", [128, D], BF16) for i in range(2)]
        kt = [sbc(f"kt{i}", [128, D], BF16) for i in range(2)]
        vt = [sbc(f"vt{i}", [128, D], BF16) for i in range(2)]
        sc_sb = sbc("sc_sb", [128, 256], F32)
        p_sb = sbc("p_sb", [128, 256], F32)
        pbf = sbc("pbf", [128, 256], BF16)
        den_sb = sbc("den_sb", [128, 16], F32)
        rden_sb = sbc("rden_sb", [128, 16], F32)
        NPROD = 4
        prod = [sbc(f"prod{i}", [128, 16 * 128], BF16) for i in range(NPROD)]
        vre = sbc("vre", [128, 2048], BF16)
        red32 = sbc("red32", [128, 2048], F32)
        arow = sbc("arow", [128, D], BF16)
        aTs = [sbc(f"aTs{i}", [128, 16 * 128], BF16) for i in range(2)]
        wo_bf = sbc("wo_bf", [128, 16 * D], BF16)
        yrow_t = sbc("yrow_t", [128, D], BF16)
        xr4 = [sbc(f"xr4_{i}", [128, D], F32) for i in range(2)]
        y1t = xr4
        xn2bf = [sbc(f"xn2bf{i}", [128, D], BF16) for i in range(2)]
        sqd4 = xn2bf
        xst_sb = [sbc(f"xst_sb{i}", [128, 2048], BF16) for i in range(1)] * 2

        pc_ps = contextlib.ExitStack()
        ps_tpc = pc_ps.enter_context(nc.psum_tensor("ps_tpc", [128, 2048], BF16))
        ps_tpd = pc_ps.enter_context(nc.psum_tensor("ps_tpd", [128, 2048], BF16))
        ps_y = [pc_ps.enter_context(nc.psum_tensor(f"ps_y{i}", [128, 512], F32))
                for i in range(2)]
        M.update(qld=[0, 0], kld=[0, 0], vld=[0, 0], ang_rd=0, tab_sc=0,
                 sc_done=0, att_done=[0, 0],
                 exp_done=0, p_used=0, p_used_g=0, pbf_used_v=0,
                 scp_done_p=[0, 0], scp_done_vv=[0, 0],
                 prod_rd=[0] * NPROD, vre_used=[0, 0], vre_cp=0,
                 atg_last=0, atv_last=0, arow_cast=0,
                 arow_tp=0, att_last_cur=0, atp_ev=[0, 0], atp_ev_last=0,
                 wo_last=[0, 0], y1use_v=0,
                 yev=[0, 0], y1st=[0, 0], x4ld=[0, 0], x4use=[0, 0],
                 tpd_ev=0, tpd_unit=0, x2st=[0, 0])

        def bc_ap(t, offset, dims):
            return bass.AP(tensor=t.tensor if hasattr(t, "tensor") else t,
                           offset=offset,
                           ap=[[t.ap().ap[0][0], 128]] + [list(d) for d in dims])

        POOL_SC = 13
        POOL_AT = 12

        def phaseCD():
            for ic in range(16):
                M["wold"] = em.dma("gpsimd", "d_wo",
                                  wo_bf.ap()[:, ic * D:(ic + 1) * D],
                                  woT[ic * 128:(ic + 1) * 128, :],
                                  waits=[("tensor", M["w_use"])] if ic == 0
                                  else ())
            def stage0(st):
                bi = st % 2
                M["qld"][bi] = em.dma(
                    "scalar", "d_qr", qt[bi][:],
                    q_sc.ap()[st * 128:(st + 1) * 128, :],
                    waits=[("gpsimd", M["scp_done_p"][bi]),
                           ("vector", M["scp_done_vv"][bi]),
                           ("d_q", M["qst_done"]),
                           ("d_k", M["kst_done"]), ("d_v", M["vst_done"])])
                M["kld"][bi] = em.dma(
                    "scalar", "d_kr", kt[bi][:],
                    k_sc.ap()[st * 128:(st + 1) * 128, :],
                    waits=[("gpsimd", M["scp_done_p"][bi]),
                           ("vector", M["scp_done_vv"][bi])])
                M["vld"][bi] = em.dma(
                    "scalar", "d_vr", vt[bi][:],
                    v_sc.ap()[st * 128:(st + 1) * 128, :],
                    waits=[("gpsimd", M["vre_used"][bi])])
            def stage1(st):
                bi = st % 2
                M["x4ld"][bi] = em.dma(
                    "scalar", f"d_x{bi}", xr4[bi][:],
                    x_own[st * 128:(st + 1) * 128, :],
                    waits=[("scalar", M["x4use"][bi]),
                           ("d_y1", M["y1st"][bi])])
                kr3 = kt[bi].ap().rearrange("p (t d) -> p t d", t=16)
                pm = [("gpsimd", 0)] * NPROD
                scpg = scpv = 0
                for hh in range(H):
                    pb = hh % NPROD
                    peng = "gpsimd" if hh < POOL_SC else "vector"
                    qbc = bc_ap(qt[bi], hh * 128, [[0, 16], [1, 128]])
                    pv = em.do(
                        peng, "tensor_tensor",
                        prod[pb].ap().rearrange("p (t d) -> p t d", t=16),
                        qbc, kr3, ALU.mult,
                        waits=[("d_qr", M["qld"][bi]), ("d_kr", M["kld"][bi]),
                               ("vector", M["prod_rd"][pb])])
                    pm[pb] = (peng, pv)
                    if peng == "gpsimd":
                        scpg = pv
                    else:
                        scpv = pv
                    M["prod_rd"][pb] = em.do(
                        "vector", "tensor_reduce",
                        sc_sb[:, hh * H:(hh + 1) * H],
                        prod[pb].ap().rearrange("p (t d) -> p t d", t=16),
                        axis=AXX, op=ALU.add,
                        waits=[pm[pb], ("scalar", M["exp_done"])])
                last_sc = max(M["prod_rd"])
                M["sc_done"] = last_sc
                M["vre_used"][bi] = M["vre_cp"] = em.do(
                    "gpsimd", "tensor_copy", vre[:],
                    bc_ap(vt[bi], 0, [[1, 128], [128, 16]]),
                    waits=[("d_vr", M["vld"][bi]),
                           ("gpsimd", M["atg_last"]),
                           ("vector", M["atv_last"])])
                ex = em.do("scalar", "activation", p_sb[:], sc_sb[:], AF.Exp,
                           scale=INV_SQRT_HD,
                           waits=[("vector", last_sc),
                                  ("vector", M["p_used"])])
                M["exp_done"] = ex
                dn = em.do("vector", "tensor_reduce", den_sb[:],
                           p_sb.ap().rearrange("p (hh t) -> p hh t", hh=H),
                           axis=AXX, op=ALU.add, waits=[("scalar", ex)])
                rd = em.do("vector", "reciprocal", rden_sb[:], den_sb[:],
                           waits=[("vector", dn)])
                p3 = p_sb.ap().rearrange("p (hh t) -> p hh t", hh=H)
                pb3 = pbf.ap().rearrange("p (hh t) -> p hh t", hh=H)
                pn = em.do("vector", "scalar_tensor_tensor", pb3, p3, 1.0,
                           bc_ap(rden_sb, 0, [[1, 16], [0, 16]]),
                           ALU.mult, ALU.mult,
                           waits=[("vector", rd), ("gpsimd", M["p_used_g"]),
                                  ("vector", M["pbf_used_v"])])
                atg = atv = 0
                red_last = pn
                for hh in range(H):
                    pb = hh % NPROD
                    peng = "gpsimd" if hh < POOL_AT else "vector"
                    pbc = bc_ap(pbf, hh * 16, [[0, 128], [1, 16]])
                    vv = vre.ap().rearrange("p (d t) -> p d t", d=128)
                    pv = em.do(
                        peng, "tensor_tensor",
                        prod[pb].ap().rearrange("p (d t) -> p d t", d=128),
                        pbc, vv, ALU.mult,
                        waits=[("vector", pn), ("vector", M["prod_rd"][pb]),
                               ("gpsimd", M["vre_cp"])])
                    if peng == "gpsimd":
                        atg = pv
                    else:
                        atv = pv
                    M["prod_rd"][pb] = red_last = em.do(
                        "vector", "tensor_reduce",
                        red32[:, hh * 128:(hh + 1) * 128],
                        prod[pb].ap().rearrange("p (d t) -> p d t", d=128),
                        axis=AXX, op=ALU.add,
                        waits=[(peng, pv), ("scalar", M["arow_cast"])])
                acast = em.do("scalar", "activation", arow[:], red32[:],
                              AF.Identity,
                              waits=[("vector", red_last),
                                     ("tensor", M["arow_tp"])])
                M["arow_cast"] = acast
                att_last = red_last
                M["p_used"] = pn
                M["p_used_g"] = atg
                M["pbf_used_v"] = atv
                M["atg_last"] = atg
                M["atv_last"] = atv
                M["scp_done_p"][bi] = max(scpg, atg)
                M["scp_done_vv"][bi] = max(scpv, atv)
                M["att_done"][bi] = att_last
                M["att_last_cur"] = acast
                conv_step(6)

            def stage1b(st):
                bi = st % 2
                items = [(ps_tpc[:, dc * 128:(dc + 1) * 128],
                          arow[:, dc * 128:(dc + 1) * 128], ident_s[:])
                         for dc in range(16)]
                tp = em.transposes(items, waits=[
                    ("scalar", M["att_last_cur"]),
                    ("scalar", M["atp_ev_last"])])
                M["arow_tp"] = tp
                ev = em.do("scalar", "activation", aTs[bi][:], ps_tpc[:],
                           AF.Copy,
                           waits=[("tensor", tp), ("tensor", M["wo_last"][bi])])
                M["atp_ev"][bi] = ev
                M["atp_ev_last"] = ev

            def stage2(s):
                bi = s % 2
                ev4 = 0
                last_womm = 0
                for oc in range(4):
                    mm = em.mms(
                        [((ps_y[oc % 2][:],
                           aTs[bi][:, ic * 128:(ic + 1) * 128],
                           wo_bf[:, ic * D + oc * 512:ic * D + (oc + 1) * 512]),
                          dict(start=(ic == 0), stop=(ic == 15)))
                         for ic in range(16)],
                        waits=[("d_wo", M["wold"]), ("scalar", M["atp_ev"][bi]),
                               ("scalar", M["yev"][oc % 2])])
                    last_womm = mm
                    ev4 = em.do("scalar", "activation",
                                yrow_t[:, oc * 512:(oc + 1) * 512],
                                ps_y[oc % 2][:], AF.Identity,
                                waits=[("tensor", mm),
                                       ("vector", M["y1use_v"])])
                    M["yev"][oc % 2] = ev4
                M["wo_last"][bi] = last_womm
                y1 = em.do("vector", "tensor_tensor", xr4[bi][:], xr4[bi][:],
                           yrow_t[:], ALU.add,
                           waits=[("scalar", ev4),
                                  (f"d_x{bi}", M["x4ld"][bi])])
                M["y1use_v"] = y1
                M["y1st"][bi] = em.dma(
                    "scalar", "d_y1", y1_sc.ap()[s * 128:(s + 1) * 128, :],
                    y1t[bi][:], waits=[("vector", y1)])
                apply = emit_ln_tile(y1t[bi][:], 4 + s % 2, xn2bf[bi], sqd4[bi],
                                     [("vector", y1), ("tensor", M["tpd_unit"])])
                M["x4use"][bi] = apply
                items = [(ps_tpd[:, dc * 128:(dc + 1) * 128],
                          xn2bf[bi][:, dc * 128:(dc + 1) * 128], ident_s[:])
                         for dc in range(16)]
                tp2 = em.transposes(items, waits=[
                    ("scalar", apply), ("scalar", M["tpd_ev"])])
                M["tpd_unit"] = tp2
                ev2 = em.do("scalar", "activation", xst_sb[bi][:], ps_tpd[:],
                            AF.Copy,
                            waits=[("tensor", tp2), ("d_x2", M["x2st"][0]),
                                   ("d_x2", M["x2st"][1])])
                M["tpd_ev"] = ev2
                M["x2st"][bi] = em.dma(
                    "scalar", "d_x2",
                    xn2_sc.ap()[:, :, s * 128:(s + 1) * 128].rearrange(
                        "dc p sl -> p dc sl"),
                    xst_sb[bi].ap().rearrange("p (dc sl) -> p dc sl", dc=16),
                    waits=[("scalar", ev2)])
                if debug:
                    em.dma("gpsimd", "d_x2",
                           dbg["y1"][s * 128:(s + 1) * 128, :], y1t[bi][:],
                           waits=[("vector", y1)])
                conv_step(6)

            stage0(0)
            for it in range(NT + 1):
                if it + 1 < NT:
                    stage0(it + 1)
                if it < NT:
                    stage1(it)
                if it >= 1:
                    stage2(it - 1)
                if it < NT:
                    stage1b(it)
            M["xn2_done"] = M["tpd_ev"]

        run_phase(phaseCD)
        barrier(["d_qr", "d_kr", "d_vr", "d_wo", "d_x0", "d_x1", "d_y1"])
        pc_ps.close()
        pc_sb.close()

        pe_sb = contextlib.ExitStack()
        sbe = lambda *a: pe_sb.enter_context(nc.sbuf_tensor(*a))
        xn2q = sbe("xn2q", [128, 16 * 512], BF16)
        h_sb = sbe("h_sb", [128, 64 * 512], BF16)
        yf_sb = sbe("yf_sb", [128, 16 * 512], BF16)
        gu_sb = sbe("gu_sb", [128, 512], BF16)
        uu_sb = sbe("uu_sb", [128, 512], BF16)
        wgb = [sbe(f"wgb{i}", [128, 16 * 128], BF16) for i in range(2)]
        wub = [sbe(f"wub{i}", [128, 16 * 128], BF16) for i in range(2)]
        wdb = [sbe(f"wdb{i}", [128, 32 * 128], BF16) for i in range(2)]
        y1r = [sbe(f"y1r{i}", [128, D], F32) for i in range(2)]
        outt = [sbe(f"outt{i}", [128, D], F32) for i in range(2)]

        pe_ps = contextlib.ExitStack()
        pste = lambda *a: pe_ps.enter_context(nc.psum_tensor(*a))
        ps_g = pste("ps_g", [128, 512], F32)
        ps_u = pste("ps_u", [128, 512], F32)
        ps_dn = [pste(f"ps_dn{i}", [128, 512], F32) for i in range(2)]
        ps_tpe = pste("ps_tpe", [128, 2048], BF16)

        M.update(g_ev=0, u_ev=0, hmul=0, wg_use=[0, 0], wd_use=[0, 0],
                 wu_use=[0, 0], xn2q_use=0,
                 dn_ev=[0, 0], yf_last=0, yftp=0, ytp_rd=0, h_used=0,
                 y1ld=[0, 0], out_tt=0, outst=[0, 0], yf_used=0)

        def phase_conv_drain():
            conv_step(N_CONV)

        run_phase(phase_conv_drain)

        def phaseE(qc):
            x2l = em.dma("sync", "d_y1r",
                         xn2q.ap().rearrange("p (dc sl) -> p dc sl", dc=16),
                         xn2_sc.ap()[:, :, qc * 512:(qc + 1) * 512].rearrange(
                             "dc p sl -> p dc sl"),
                         waits=[("tensor", M["xn2q_use"]),
                                ("d_x2", M["x2st"][0]), ("d_x2", M["x2st"][1])])
            for it in range(64):
                wb = it % 2
                wgl = em.dma("sync", "d_fg",
                             wgb[wb].ap().rearrange("p (hc i) -> p hc i", hc=16),
                             wg_sc.ap()[:, :, it * 128:(it + 1) * 128].rearrange(
                                 "hc p i -> p hc i"),
                             waits=[("tensor", M["wg_use"][wb]),
                                    ("d_fcs", M["conv_done"])])
                gmm = em.mms(
                    [((ps_g[:], wgb[wb][:, hc * 128:(hc + 1) * 128],
                       xn2q[:, hc * 512:(hc + 1) * 512]),
                      dict(start=(hc == 0), stop=(hc == 15)))
                     for hc in range(16)],
                    waits=[("d_fg", wgl), ("d_y1r", x2l),
                           ("scalar", M["g_ev"])])
                gev = em.do("scalar", "activation", gu_sb[:], ps_g[:],
                            AF.Silu, scale=gsc_s[:, it:it + 1],
                            waits=[("tensor", gmm), ("vector", M["hmul"]),
                                   ("d_c", M["consts"])])
                M["g_ev"] = gev
                wul = em.dma("sync", "d_fu",
                             wub[wb].ap().rearrange("p (hc i) -> p hc i", hc=16),
                             wu_sc.ap()[:, :, it * 128:(it + 1) * 128].rearrange(
                                 "hc p i -> p hc i"),
                             waits=[("tensor", M["wu_use"][wb]),
                                    ("d_fcs", M["conv_done"])])
                umm = em.mms(
                    [((ps_u[:], wub[wb][:, hc * 128:(hc + 1) * 128],
                       xn2q[:, hc * 512:(hc + 1) * 512]),
                      dict(start=(hc == 0), stop=(hc == 15)))
                     for hc in range(16)],
                    waits=[("d_fu", wul), ("scalar", M["u_ev"])])
                M["wg_use"][wb] = gmm
                M["wu_use"][wb] = umm
                uev = em.do("scalar", "activation", uu_sb[:], ps_u[:],
                            AF.Identity, scale=usc_s[:, it:it + 1],
                            waits=[("tensor", umm), ("vector", M["hmul"])])
                M["u_ev"] = uev
                hm = em.do("vector", "tensor_tensor",
                           h_sb[:, it * 512:(it + 1) * 512], gu_sb[:], uu_sb[:],
                           ALU.mult,
                           waits=[("scalar", uev)] +
                                 ([("tensor", M["h_used"])] if it == 0 else []))
                M["hmul"] = hm
            h_done = M["hmul"]
            dn_last = 0
            for ht in range(16):
                wb = ht % 2
                wdl1 = em.dma("sync", "d_fd",
                              wdb[wb].ap().rearrange("p (ic i) -> p ic i", ic=32),
                              wd_sc.ap()[0:32, :, ht * 128:(ht + 1) * 128].rearrange(
                                  "ic p hh -> p ic hh"),
                              waits=[("tensor", M["wd_use"][wb])])
                dmm = em.mms(
                    [((ps_dn[ht % 2][:], wdb[wb][:, ic * 128:(ic + 1) * 128],
                       h_sb[:, ic * 512:(ic + 1) * 512]),
                      dict(start=(ic == 0), stop=False, skip_group_check=True))
                     for ic in range(32)],
                    waits=[("d_fd", wdl1), ("vector", h_done),
                           ("scalar", M["dn_ev"][ht % 2])])
                wdl2 = em.dma("sync", "d_fd",
                              wdb[wb].ap().rearrange("p (ic i) -> p ic i", ic=32),
                              wd_sc.ap()[32:64, :, ht * 128:(ht + 1) * 128].rearrange(
                                  "ic p hh -> p ic hh"),
                              waits=[("tensor", dmm)])
                dmm = em.mms(
                    [((ps_dn[ht % 2][:], wdb[wb][:, ic * 128:(ic + 1) * 128],
                       h_sb[:, (32 + ic) * 512:(33 + ic) * 512]),
                      dict(start=False, stop=(ic == 31), skip_group_check=True))
                     for ic in range(32)],
                    waits=[("d_fd", wdl2)])
                M["wd_use"][wb] = dmm
                dn_last = dmm
                yfe = em.do("scalar", "activation",
                            yf_sb[:, ht * 512:(ht + 1) * 512],
                            ps_dn[ht % 2][:], AF.Identity,
                            scale=dsc_s[:, ht:ht + 1],
                            waits=[("tensor", dmm), ("tensor", M["yftp"])])
                M["dn_ev"][ht % 2] = yfe
                M["yf_last"] = yfe
            M["h_used"] = dn_last
            M["xn2q_use"] = M["wu_use"][1]
            for qs in range(4):
                items = [(ps_tpe[:, ht * 128:(ht + 1) * 128],
                          yf_sb[:, ht * 512 + qs * 128:ht * 512 + (qs + 1) * 128],
                          ident_s[:]) for ht in range(16)]
                tp = em.transposes(items, waits=[
                    ("scalar", M["yf_last"]), ("vector", M["ytp_rd"])])
                M["yftp"] = tp
                bi = qs % 2
                r0 = qc * 512 + qs * 128
                yl = em.dma("sync", "d_y1r", y1r[bi][:],
                            y1_sc.ap()[r0:r0 + 128, :],
                            waits=[("vector", M["y1ld"][bi]),
                                   ("d_y1", M["y1st"][0]),
                                   ("d_y1", M["y1st"][1])])
                ott = em.do("vector", "tensor_tensor", outt[bi][:], y1r[bi][:],
                            ps_tpe[:], ALU.add,
                            waits=[("tensor", tp), ("d_y1r", yl),
                                   ("d_out", M["outst"][bi])])
                M["y1ld"][bi] = ott
                M["ytp_rd"] = ott
                M["out_tt"] = ott
                M["outst"][bi] = em.dma(
                    "gpsimd", "d_out", out_rows[r0:r0 + 128, :], outt[bi][:],
                    waits=[("vector", ott)])

        for qc in range(NCH):
            run_phase(phaseE, qc)

        def phase_final():
            em.wait("gpsimd", [(stn, em.clk[stn]) for stn in STREAMS])
            em.wait("sync", [("d_out", em.clk["d_out"])])

        run_phase(phase_final)
        pe_ps.close()
        pe_sb.close()
        cv_sb.close()

    return nc



def _lowbyte(a):
    return np.ascontiguousarray(
        a.view(np.uint8).reshape(a.shape[0], a.shape[1], 4)[:, :, 0])


def _stage_inputs(inputs):
    import ml_dtypes
    x = np.ascontiguousarray(np.asarray(inputs["L0_attn_x"], dtype=np.float32)[0])
    wq = np.asarray(inputs["L0_attn_wq"], dtype=np.float32)
    wk = np.asarray(inputs["L0_attn_wk"], dtype=np.float32)
    wv = np.asarray(inputs["L0_attn_wv"], dtype=np.float32)
    wo = np.asarray(inputs["L0_attn_wo"], dtype=np.float32)
    inv_freq = np.asarray(inputs["L0_attn_inv_freq"], dtype=np.float32)
    gw = np.asarray(inputs["L0_ffn_gate_w"], dtype=np.int32)
    gs = np.asarray(inputs["L0_ffn_gate_scale"], dtype=np.float32)
    uw = np.asarray(inputs["L0_ffn_up_w"], dtype=np.int32)
    us = np.asarray(inputs["L0_ffn_up_scale"], dtype=np.float32)
    dw = np.asarray(inputs["L0_ffn_down_w"], dtype=np.int32)
    ds = np.asarray(inputs["L0_ffn_down_scale"], dtype=np.float32)

    perm = np.concatenate([np.arange(0, HD, 2), np.arange(1, HD, 2)])
    perm_g = (np.arange(H)[:, None] * HD + perm[None, :]).reshape(-1)

    common = {
        "wkT_p": np.ascontiguousarray(wk.T[:, perm_g]),
        "wqT_p": np.ascontiguousarray(wq.T[:, perm_g]),
        "wvT": np.ascontiguousarray(wv.T),
        "woT": np.ascontiguousarray(wo.T),
        "gwT": _lowbyte(np.ascontiguousarray(gw.T)),
        "uwT": _lowbyte(np.ascontiguousarray(uw.T)),
        "dwT": _lowbyte(np.ascontiguousarray(dw.T)),
        "gscale": np.ascontiguousarray(gs.reshape(FI // 128, 128).T),
        "uscale": np.ascontiguousarray(us.reshape(FI // 128, 128).T),
        "dscale": np.ascontiguousarray(ds.reshape(D // 128, 128).T),
        "invf_rep": np.ascontiguousarray(np.tile(inv_freq, H)[None, :]),
        "ident_bf": np.eye(128).astype(ml_dtypes.bfloat16),
        "onesc": np.ascontiguousarray(np.broadcast_to(
            np.array([1.0, LN_EPS, 1e-37, PI_HALF], np.float32), (128, 4))),
    }
    per_core = []
    for c in range(8):
        m = dict(common)
        m["x_own"] = np.ascontiguousarray(x[c * S_OWN:(c + 1) * S_OWN])
        m["t_ownT"] = np.arange(c * S_OWN, (c + 1) * S_OWN,
                                dtype=np.float32)[None, :].copy()
        per_core.append(m)
    return per_core


def kernel(**inputs):
    in_maps = _stage_inputs(inputs)
    nc = build(debug=False)
    res = run_bass_kernel_spmd(nc, in_maps, core_ids=list(range(8)))
    out = np.concatenate([res.results[c]["out_rows"] for c in range(8)], axis=0)
    return out[None, :, :]
